# version 01 (final)
# baseline (speedup 1.0000x reference)
import numpy as np
import ml_dtypes
import concourse.bass as bass
import concourse.mybir as mybir
from concourse.bass_utils import run_bass_kernel_spmd

F32 = mybir.dt.float32
BF16 = mybir.dt.bfloat16
AF = mybir.ActivationFunctionType
AL = mybir.AluOpType
AX = mybir.AxisListType
ET = mybir.EngineType
BF = ml_dtypes.bfloat16

NCORES = 8
T = 2048
D = 1024
NT = T // 128
ND = D // 128
NEG = -1.0e30


class PB:

    def __init__(self):
        self.nc = bass.Bass("TRN2", target_bir_lowering=False)
        self.sg = self.nc.alloc_semaphore("sg")
        self.sd = self.nc.alloc_semaphore("sd")
        self.ng = 0
        self.nd = 0
        self.bar_g = 0
        self.bar_d = 0
        self.waited = set()

    def bar(self):
        if self.bar_g == self.ng and self.bar_d == self.nd:
            return
        self.bar_g = self.ng
        self.bar_d = self.nd
        self.waited = set()

    def _pre(self, en):
        if en not in self.waited:
            e = getattr(self.nc, en)
            if self.bar_g:
                e.wait_ge(self.sg, self.bar_g)
            if self.bar_d:
                e.wait_ge(self.sd, 16 * self.bar_d)
            self.waited.add(en)

    def _emit(self, en, fn):
        self._pre(en)
        fn(getattr(self.nc, en)).then_inc(self.sg, 1)
        self.ng += 1

    def t(self, fn):
        self._emit("tensor", fn)

    def v(self, fn):
        self._emit("vector", fn)

    def a(self, fn):
        self._emit("scalar", fn)

    def g(self, fn):
        self._emit("gpsimd", fn)

    def dma(self, out, in_, **kw):
        self._pre("sync")
        self.nc.sync.dma_start(out, in_, **kw).then_inc(self.sd, 16)
        self.nd += 1

    def fin(self):
        self.bar()
        self._pre("sync")
        self.nc.finalize()
        return self.nc


def mm(p, out, lhsT, rhs, start, stop):
    p.t(lambda e: e.matmul(out, lhsT, rhs, start=start, stop=stop,
                           skip_group_check=True))


def transposes(p, items, pt0, pt1, ident):
    pend = None
    for i, (src, dst) in enumerate(items):
        pt = (pt0, pt1)[i % 2]
        ph, fh = src.shape[0], src.shape[1]
        p.t(lambda e, s=src, q=pt, f=fh, pp=ph: e.transpose(q[0:f, 0:pp], s, ident))
        if pend is not None:
            q0, d0, f0, p0 = pend
            p.a(lambda e, q=q0, d=d0, f=f0, pp=p0: e.copy(d, q[0:f, 0:pp]))
        pend = (pt, dst, fh, ph)
        p.bar()
    if pend is not None:
        q0, d0, f0, p0 = pend
        p.a(lambda e, q=q0, d=d0, f=f0, pp=p0: e.copy(d, q[0:f, 0:pp]))
        p.bar()


def bcast_row(p, row, out_sb, ones_row, psum, n):
    for c0 in range(0, n, 512):
        w = min(512, n - c0)
        mm(p, psum[:, 0:w], ones_row, row[:, c0:c0 + w], True, True)
        p.bar()
        p.a(lambda e, a=c0, ww=w: e.copy(out_sb[:, a:a + ww], psum[:, 0:ww]))
        p.bar()


def small_scratch(p):
    nc = p.nc
    sc = {}
    sc["c0"] = nc.alloc_sbuf_tensor("sc_c0", [128, 4], F32)
    sc["c1"] = nc.alloc_sbuf_tensor("sc_c1", [128, 4], F32)
    sc["c2"] = nc.alloc_sbuf_tensor("sc_c2", [128, 4], F32)
    sc["c3"] = nc.alloc_sbuf_tensor("sc_c3", [128, 4], F32)
    sc["junk"] = nc.alloc_sbuf_tensor("sc_junk", [128, 1024], BF16)
    sc["sqf"] = nc.alloc_sbuf_tensor("sc_sqf", [128, 1024], F32)
    sc["lnbf"] = nc.alloc_sbuf_tensor("sc_lnbf", [128, 1024], BF16)
    idb = nc.alloc_sbuf_tensor("idbf_t", [128, 128], BF16)
    idf = nc.alloc_sbuf_tensor("idf_t", [128, 128], F32)
    sc["idbf"] = idb[:]
    sc["idf"] = idf[:]
    dib = nc.dram_tensor("c_idbf", [128, 128], BF16, kind="ExternalInput").ap()
    dif = nc.dram_tensor("c_idf", [128, 128], F32, kind="ExternalInput").ap()
    p.dma(idb[:], dib)
    p.dma(idf[:], dif)
    p.bar()
    return sc


def emit_ln(p, xd, lnT, xb, sc, pt0, pt1, rms_extra=False):
    for tt in range(NT):
        xbuf = xb[tt % 2]
        p.dma(xbuf[:], xd[tt * 128:(tt + 1) * 128, :])
        p.bar()
        p.v(lambda e, b=xbuf: e.reduce_sum(sc["c0"][:, 0:1], b[:], axis=AX.X))
        p.bar()
        p.v(lambda e: e.tensor_scalar_mul(sc["c0"][:, 0:1], sc["c0"][:, 0:1], 1.0 / D))
        p.bar()
        p.v(lambda e, b=xbuf: e.tensor_scalar(b[:], b[:], sc["c0"][:, 0:1], None,
                                              AL.subtract))
        p.bar()
        p.a(lambda e, b=xbuf: e.activation(sc["sqf"][:], b[:], AF.Square))
        p.bar()
        p.v(lambda e: e.reduce_sum(sc["c1"][:, 0:1], sc["sqf"][:], axis=AX.X))
        p.bar()
        p.v(lambda e: e.tensor_scalar(sc["c1"][:, 0:1], sc["c1"][:, 0:1],
                                      1.0 / D, 1e-5, AL.mult, AL.add))
        p.bar()
        p.a(lambda e: e.activation(sc["c2"][:, 0:1], sc["c1"][:, 0:1], AF.Ln))
        p.bar()
        p.a(lambda e: e.activation(sc["c2"][:, 0:1], sc["c2"][:, 0:1], AF.Exp,
                                   scale=-0.5))
        p.bar()
        p.v(lambda e, b=xbuf: e.tensor_scalar(b[:], b[:], sc["c2"][:, 0:1], None,
                                              AL.mult))
        p.bar()
        if rms_extra:
            p.a(lambda e, b=xbuf: e.activation(sc["sqf"][:], b[:], AF.Square))
            p.bar()
            p.v(lambda e: e.reduce_sum(sc["c1"][:, 0:1], sc["sqf"][:], axis=AX.X))
            p.bar()
            p.a(lambda e: e.activation(sc["c2"][:, 0:1], sc["c1"][:, 0:1], AF.Sqrt,
                                       scale=1.0 / D))
            p.bar()
            p.v(lambda e: e.tensor_scalar_add(sc["c2"][:, 0:1], sc["c2"][:, 0:1], 1e-8))
            p.bar()
            p.v(lambda e: e.reciprocal(sc["c3"][:, 0:1], sc["c2"][:, 0:1]))
            p.bar()
            p.v(lambda e, b=xbuf: e.tensor_scalar(b[:], b[:], sc["c3"][:, 0:1], None,
                                                  AL.mult))
            p.bar()
        p.v(lambda e, b=xbuf: e.tensor_copy(sc["lnbf"][:], b[:]))
        p.bar()
        items = []
        for dc in range(ND):
            items.append((sc["lnbf"][:, dc * 128:(dc + 1) * 128],
                          lnT[:, dc * T + tt * 128: dc * T + (tt + 1) * 128]))
        transposes(p, items, pt0, pt1, sc["idbf"])


def lin_oT(p, w_sb, rhs, out_bf, psA, psB, nout, bias_col=None, nk=ND, act=None):
    flip = 0
    for ot in range(nout // 128):
        for tb in range(T // 512):
            ps = (psA, psB)[flip % 2]
            flip += 1
            for k in range(nk):
                mm(p, ps[:, 0:512],
                   w_sb[:, k * nout + ot * 128: k * nout + (ot + 1) * 128],
                   rhs[:, k * T + tb * 512: k * T + tb * 512 + 512],
                   k == 0, k == nk - 1)
            p.bar()
            dst = out_bf[:, ot * T + tb * 512: ot * T + tb * 512 + 512]
            if act is not None:
                bc = bias_col[:, ot:ot + 1] if bias_col is not None else 0.0
                p.a(lambda e, d=dst, q=ps, b=bc: e.activation(d, q[:, 0:512], act,
                                                              bias=b))
            elif bias_col is not None:
                p.v(lambda e, d=dst, q=ps, b=bias_col[:, ot:ot + 1]:
                    e.tensor_scalar(d, q[:, 0:512], b, None, AL.add))
            else:
                p.a(lambda e, d=dst, q=ps: e.copy(d, q[:, 0:512]))
            p.bar()


def lin_tm(p, w_sb, lnT, out_bf, psA, psB, nout, nk=ND, bias_bc=None, evac=None):
    flip = 0
    for tt in range(NT):
        for ob in range(0, nout, 512):
            ow = min(512, nout - ob)
            ps = (psA, psB)[flip % 2]
            flip += 1
            for k in range(nk):
                mm(p, ps[:, 0:ow],
                   lnT[:, k * T + tt * 128: k * T + (tt + 1) * 128],
                   w_sb[:, k * nout + ob: k * nout + ob + ow],
                   k == 0, k == nk - 1)
            p.bar()
            dst = out_bf[:, tt * nout + ob: tt * nout + ob + ow]
            if evac is not None:
                evac(tt, ob, ow, dst, ps)
            elif bias_bc is not None:
                p.v(lambda e, d=dst, q=ps, o=ob, w2=ow:
                    e.tensor_tensor(d, q[:, 0:w2], bias_bc[:, o:o + w2], AL.add))
            else:
                p.a(lambda e, d=dst, q=ps, w2=ow: e.copy(d, q[:, 0:w2]))
            p.bar()


def load_w(p, dram, w_sb, rows, cols):
    for k in range(rows // 128):
        p.dma(w_sb[:, k * cols:(k + 1) * cols], dram[k * 128:(k + 1) * 128, :])
    p.bar()


def _row(p, dram_row, row_sb, n):
    p.dma(row_sb[0:1, 0:n], dram_row[0:1, 0:n])
    p.bar()
    return row_sb[0:1, 0:n]



CONST_NP = {}


def get_consts():
    if CONST_NP:
        return CONST_NP
    CONST_NP["c_idbf"] = np.eye(128, dtype=np.float32).astype(BF)
    CONST_NP["c_idf"] = np.eye(128, dtype=np.float32)
    CONST_NP["c_ones_col_bf"] = np.ones((128, 1), dtype=np.float32).astype(BF)
    CONST_NP["c_ones_col_f"] = np.ones((128, 1), dtype=np.float32)
    CONST_NP["c_ones_row_f"] = np.ones((1, 128), dtype=np.float32)
    Dh = 64
    inv = 1.0 / (10000.0 ** (np.arange(0, Dh, 2, dtype=np.float64) / Dh))
    fr = np.outer(np.arange(T, dtype=np.float64), inv)
    CONST_NP["c_cosrep"] = np.tile(np.cos(fr), (1, 16)).astype(np.float32)
    CONST_NP["c_sinrep"] = np.tile(np.sin(fr), (1, 16)).astype(np.float32)
    cm = np.zeros((4, 128, 512), dtype=np.float32)
    for di, dlt in enumerate((0, 128, 256, 384)):
        k = np.arange(128)[:, None] + dlt
        j = np.arange(512)[None, :]
        cm[di] = np.where(k > j, NEG, 0.0)
    CONST_NP["c_cmask"] = cm.reshape(512, 512).copy()
    return CONST_NP



PREP_NROWS = None


def build_prep(nrows):
    p = PB()
    nc = p.nc
    fin = nc.dram_tensor("flat_in", [nrows, 4096], F32, kind="ExternalInput").ap()
    fout = nc.dram_tensor("flat_bf", [nrows, 4096], BF16, kind="ExternalOutput").ap()
    tin = nc.dram_tensor("tern_in", [2048, 4096], F32, kind="ExternalInput").ap()
    tout = nc.dram_tensor("tern_bf", [2048, 4096], BF16, kind="ExternalOutput").ap()
    orow = nc.dram_tensor("c_ones_row_f", [1, 128], F32, kind="ExternalInput").ap()
    ocol = nc.dram_tensor("c_ones_col_f", [128, 1], F32, kind="ExternalInput").ap()

    a = nc.alloc_sbuf_tensor("pa", [128, 4096], F32)
    b = nc.alloc_sbuf_tensor("pb", [128, 4096], F32)
    obf = nc.alloc_sbuf_tensor("pobf", [128, 4096], BF16)
    obf2 = nc.alloc_sbuf_tensor("pobf2", [128, 4096], BF16)
    onesr = nc.alloc_sbuf_tensor("onesr", [1, 128], F32)
    onesc = nc.alloc_sbuf_tensor("onesc", [128, 1], F32)
    acc = nc.alloc_sbuf_tensor("acc", [128, 4], F32)
    junk = nc.alloc_sbuf_tensor("pjunk", [128, 4096], BF16)
    alph = nc.alloc_sbuf_tensor("alph", [1, 4], F32)
    alphc = nc.alloc_sbuf_tensor("alphc", [128, 4], F32)
    pos = nc.alloc_sbuf_tensor("pos", [128, 4096], F32)
    ps1 = nc.alloc_psum_tensor("ps1", [128, 4], F32)
    ps2 = nc.alloc_psum_tensor("ps2", [1, 4], F32)

    p.dma(onesr[:], orow)
    p.dma(onesc[:], ocol)
    p.bar()
    for i in range(nrows // 128):
        buf = (a, b)[i % 2]
        ob = (obf, obf2)[i % 2]
        p.dma(buf[:], fin[i * 128:(i + 1) * 128, :])
        p.bar()
        p.v(lambda e, bu=buf, o=ob: e.tensor_copy(o[:], bu[:]))
        p.bar()
        p.dma(fout[i * 128:(i + 1) * 128, :], ob[:])
        p.bar()
    for half in range(2):
        p.g(lambda e: e.memset(acc[:, 0:1], 0.0))
        p.bar()
        for i in range(8):
            buf = (a, b)[i % 2]
            p.dma(buf[:], tin[half * 1024 + i * 128: half * 1024 + (i + 1) * 128, :])
            p.bar()
            p.a(lambda e, bu=buf: e.activation(pos[:], bu[:], AF.Abs))
            p.bar()
            p.v(lambda e: e.reduce_sum(acc[:, 1:2], pos[:], axis=AX.X))
            p.bar()
            p.v(lambda e: e.tensor_add(acc[:, 0:1], acc[:, 0:1], acc[:, 1:2]))
            p.bar()
        mm(p, ps2[:, 0:1], onesc[:], acc[:, 0:1], True, True)
        p.bar()
        p.v(lambda e: e.tensor_scalar_mul(alph[:, 0:1], ps2[:, 0:1],
                                          1.0 / (1024.0 * 4096.0)))
        p.bar()
        mm(p, ps1[:, 0:1], onesr[:], alph[:, 0:1], True, True)
        p.bar()
        p.a(lambda e: e.copy(alphc[:, 0:1], ps1[:, 0:1]))
        p.bar()
        p.v(lambda e: e.tensor_scalar_mul(alphc[:, 1:2], alphc[:, 0:1], 0.5))
        p.v(lambda e: e.tensor_scalar_mul(alphc[:, 2:3], alphc[:, 0:1], -0.5))
        p.bar()
        for i in range(8):
            buf = (a, b)[i % 2]
            p.dma(buf[:], tin[half * 1024 + i * 128: half * 1024 + (i + 1) * 128, :])
            p.bar()
            p.v(lambda e, bu=buf: e.tensor_scalar(pos[:], bu[:], alphc[:, 1:2],
                                                  None, AL.is_gt))
            p.bar()
            p.v(lambda e, bu=buf: e.tensor_scalar(bu[:], bu[:], alphc[:, 2:3],
                                                  None, AL.is_lt))
            p.bar()
            p.v(lambda e, bu=buf: e.tensor_sub(obf[:], pos[:], bu[:]))
            p.bar()
            p.dma(tout[half * 1024 + i * 128: half * 1024 + (i + 1) * 128, :], obf[:])
            p.bar()
    return p.fin()



def _rope_tile(p, src, dst, nh, cosrep, sinrep, rp):
    W = nh * 32
    ev = src.rearrange("p (h a two) -> p h a two", h=nh, two=2)[:, :, :, 0:1]
    od = src.rearrange("p (h a two) -> p h a two", h=nh, two=2)[:, :, :, 1:2]
    cr = cosrep.rearrange("p (h a one) -> p h a one", h=nh, one=1)
    sr = sinrep.rearrange("p (h a one) -> p h a one", h=nh, one=1)

    def r4(ap2):
        return ap2.rearrange("p (h a one) -> p h a one", h=nh, one=1)

    p.v(lambda e: e.tensor_mul(r4(rp[:, 0 * W:1 * W]), ev, cr))
    p.v(lambda e: e.tensor_mul(r4(rp[:, 1 * W:2 * W]), od, sr))
    p.v(lambda e: e.tensor_mul(r4(rp[:, 2 * W:3 * W]), ev, sr))
    p.v(lambda e: e.tensor_mul(r4(rp[:, 3 * W:4 * W]), od, cr))
    p.bar()
    d1 = dst.rearrange("p (h c) -> p h c", h=nh)[:, :, 0:32]
    d2 = dst.rearrange("p (h c) -> p h c", h=nh)[:, :, 32:64]
    p.v(lambda e: e.tensor_tensor(
        d1, rp[:, 0 * W:1 * W].rearrange("p (h a) -> p h a", h=nh),
        rp[:, 1 * W:2 * W].rearrange("p (h a) -> p h a", h=nh), AL.subtract))
    p.v(lambda e: e.tensor_tensor(
        d2, rp[:, 2 * W:3 * W].rearrange("p (h a) -> p h a", h=nh),
        rp[:, 3 * W:4 * W].rearrange("p (h a) -> p h a", h=nh), AL.add))
    p.bar()


def build_att(layer):
    p = PB()
    nc = p.nc
    xd = nc.dram_tensor("x", [T, D], F32, kind="ExternalInput").ap()
    xo = nc.dram_tensor("x_out", [T, D], F32, kind="ExternalOutput").ap()
    sc = small_scratch(p)

    kv_rows = 256 if layer == 3 else D
    wq = nc.dram_tensor("wq", [D, D], BF16, kind="ExternalInput").ap()
    wk = nc.dram_tensor("wk", [D, kv_rows], BF16, kind="ExternalInput").ap()
    wv = nc.dram_tensor("wv", [D, kv_rows], BF16, kind="ExternalInput").ap()
    has_o = layer in (0, 3)
    if has_o:
        wo = nc.dram_tensor("wo", [D, D], BF16, kind="ExternalInput").ap()
    if layer == 0:
        bqc_d = nc.dram_tensor("bq", [128, 8], F32, kind="ExternalInput").ap()
        bkc_d = nc.dram_tensor("bk", [128, 8], F32, kind="ExternalInput").ap()
        bvr_d = nc.dram_tensor("bv", [1, D], F32, kind="ExternalInput").ap()
        bor_d = nc.dram_tensor("bo", [1, D], F32, kind="ExternalInput").ap()
    if layer == 1:
        bqr_d = nc.dram_tensor("bq", [1, D], F32, kind="ExternalInput").ap()
        bkc_d = nc.dram_tensor("bk", [128, 8], F32, kind="ExternalInput").ap()
        bvr_d = nc.dram_tensor("bv", [1, D], F32, kind="ExternalInput").ap()
        ws_d = nc.dram_tensor("ws", [D, 256], BF16, kind="ExternalInput").ap()
        bsr_d = nc.dram_tensor("bs", [1, 256], F32, kind="ExternalInput").ap()
    if layer == 3:
        cosr_d = nc.dram_tensor("c_cosrep", [T, 512], F32, kind="ExternalInput").ap()
        sinr_d = nc.dram_tensor("c_sinrep", [T, 512], F32, kind="ExternalInput").ap()
        cmd = nc.dram_tensor("c_cmask", [512, 512], F32, kind="ExternalInput").ap()
    orow_d = nc.dram_tensor("c_ones_row_f", [1, 128], F32, kind="ExternalInput").ap()
    ocbf_d = nc.dram_tensor("c_ones_col_bf", [128, 1], BF16, kind="ExternalInput").ap()

    nkv_t = 2 if layer == 3 else ND
    nv = 256 if layer == 3 else D
    B0 = nc.alloc_sbuf_tensor("B0", [128, ND * T], BF16)
    B1 = nc.alloc_sbuf_tensor("B1", [128, ND * T], BF16)
    B2 = nc.alloc_sbuf_tensor("B2", [128, nkv_t * T], BF16)
    B3 = nc.alloc_sbuf_tensor("B3", [128, NT * nv], BF16)
    if layer == 3:
        QS = nc.alloc_sbuf_tensor("QS", [128, NT * D], BF16)
        B2d = nc.alloc_sbuf_tensor("B2d", [128, 4 * T], BF16)
    wbuf = nc.alloc_sbuf_tensor("wbuf", [128, ND * D], BF16)
    ebuf = wbuf
    xb = [nc.alloc_sbuf_tensor("xbA", [128, D], F32),
          nc.alloc_sbuf_tensor("xbB", [128, D], F32)]
    onesr = nc.alloc_sbuf_tensor("onesr", [1, 128], F32)
    onescb = nc.alloc_sbuf_tensor("onescb", [128, 1], BF16)
    den = nc.alloc_sbuf_tensor("den", [1, T], F32)
    denc = nc.alloc_sbuf_tensor("denc", [128, 16], F32)
    rec = nc.alloc_sbuf_tensor("rec", [128, 16], F32)
    bb = nc.alloc_sbuf_tensor("bb", [128, D], F32)
    rowb = nc.alloc_sbuf_tensor("rowb", [1, D], F32)
    colb = nc.alloc_sbuf_tensor("colb", [128, 16], F32)
    sm = nc.alloc_sbuf_tensor("sm", [128, 512], F32)

    psA = nc.alloc_psum_tensor("psA", [128, 512], F32)
    psB = nc.alloc_psum_tensor("psB", [128, 512], F32)
    psC = nc.alloc_psum_tensor("psC", [128, 512], F32)
    pout = nc.alloc_psum_tensor("pout", [128, 512], F32)
    prow = nc.alloc_psum_tensor("prow", [1, 512], F32)
    ptb0 = nc.alloc_psum_tensor("ptb0", [128, 128], BF16)
    ptb1 = nc.alloc_psum_tensor("ptb1", [128, 128], BF16)
    ptf = nc.alloc_psum_tensor("ptf", [128, 128], F32)

    p.dma(onesr[:], orow_d)
    p.dma(onescb[:], ocbf_d)
    p.bar()

    emit_ln(p, xd, B0[:], xb, sc, ptb0, ptb1)
    nh = 16

    if layer in (0, 2):
        load_w(p, wq, wbuf[:], D, D)
        bcq = None
        if layer == 0:
            p.dma(colb[:, 0:8], bqc_d)
            p.bar()
            bcq = colb[:, 0:8]
        lin_oT(p, wbuf[:], B0[:], B1[:], psA, psB, D, bias_col=bcq)
        load_w(p, wk, wbuf[:], D, D)
        bck = None
        if layer == 0:
            p.dma(colb[:, 8:16], bkc_d)
            p.bar()
            bck = colb[:, 8:16]
        lin_oT(p, wbuf[:], B0[:], B2[:], psA, psB, D, bias_col=bck)
    elif layer == 1:
        load_w(p, wq, wbuf[:], D, D)
        bcast_row(p, _row(p, bqr_d, rowb, D), bb, onesr[:], psA, D)
        lin_tm(p, wbuf[:], B0[:], B3[:], psA, psB, D, bias_bc=bb[:])
        load_w(p, ws_d, wbuf[:, 0:ND * 256], D, 256)
        bcast_row(p, _row(p, bsr_d, rowb, 256), bb, onesr[:], psA, 256)
        s_tm = nc.alloc_sbuf_tensor("s_tm", [128, NT * 256], BF16)
        sxp = nc.alloc_sbuf_tensor("sxp", [128, 1024], BF16)

        def evac_s(tt, ob, ow, dst, ps):
            p.v(lambda e: e.tensor_tensor(sm[:, 0:ow], ps[:, 0:ow],
                                          bb[:, ob:ob + ow], AL.add))
            p.bar()
            p.a(lambda e: e.activation(sm[:, 0:ow], sm[:, 0:ow], AF.Sigmoid))
            p.bar()
            p.v(lambda e: e.tensor_scalar(dst, sm[:, 0:ow], 0.1, 0.95,
                                          AL.mult, AL.add))
        lin_tm(p, wbuf[:, 0:ND * 256], B0[:], s_tm[:], psA, psB, 256, evac=evac_s)
        for tt in range(NT):
            src3 = s_tm[:, tt * 256:(tt + 1) * 256].rearrange(
                "p (g one) -> p g one", one=1)
            for j in range(4):
                dst3 = sxp[:].rearrange("p (g four) -> p g four", four=4)[:, :, j:j + 1]
                p.v(lambda e, d=dst3, s=src3: e.tensor_copy(d, s))
            p.bar()
            p.v(lambda e, t2=tt: e.tensor_mul(sc["lnbf"][:],
                                              B3[:, t2 * D:(t2 + 1) * D], sxp[:]))
            p.bar()
            items = []
            for dc in range(ND):
                items.append((sc["lnbf"][:, dc * 128:(dc + 1) * 128],
                              B1[:, dc * T + tt * 128: dc * T + (tt + 1) * 128]))
            transposes(p, items, ptb0, ptb1, sc["idbf"])
        load_w(p, wk, wbuf[:], D, D)
        p.dma(colb[:, 8:16], bkc_d)
        p.bar()
        lin_oT(p, wbuf[:], B0[:], B2[:], psA, psB, D, bias_col=colb[:, 8:16])
    else:
        cosb = nc.alloc_sbuf_tensor("cosb", [128, 512], F32)
        sinb = nc.alloc_sbuf_tensor("sinb", [128, 512], F32)
        rp = nc.alloc_sbuf_tensor("rp", [128, 2048], BF16)
        cmb = nc.alloc_sbuf_tensor("cmb", [128, 4 * 512], F32)
        for di in range(4):
            p.dma(cmb[:, di * 512:(di + 1) * 512], cmd[di * 128:(di + 1) * 128, :])
        p.bar()
        load_w(p, wq, wbuf[:], D, D)
        lin_tm(p, wbuf[:], B0[:], QS[:], psA, psB, D)
        load_w(p, wk, wbuf[:, 0:ND * 256], D, 256)
        ktm = B3
        lin_tm(p, wbuf[:, 0:ND * 256], B0[:], ktm[:], psA, psB, 256)
        for tt in range(NT):
            p.dma(cosb[:], cosr_d[tt * 128:(tt + 1) * 128, :])
            p.dma(sinb[:], sinr_d[tt * 128:(tt + 1) * 128, :])
            p.bar()
            _rope_tile(p, QS[:, tt * D:(tt + 1) * D], sc["lnbf"][:], 16,
                       cosb[:], sinb[:], rp)
            items = []
            for dc in range(ND):
                items.append((sc["lnbf"][:, dc * 128:(dc + 1) * 128],
                              B1[:, dc * T + tt * 128: dc * T + (tt + 1) * 128]))
            transposes(p, items, ptb0, ptb1, sc["idbf"])
            _rope_tile(p, ktm[:, tt * 256:(tt + 1) * 256], sc["lnbf"][:, 0:256], 4,
                       cosb[:, 0:128], sinb[:, 0:128], rp)
            items = []
            for dc in range(2):
                items.append((sc["lnbf"][:, dc * 128:(dc + 1) * 128],
                              B2[:, dc * T + tt * 128: dc * T + (tt + 1) * 128]))
            transposes(p, items, ptb0, ptb1, sc["idbf"])
        for kv in range(4):
            src = B2[(kv % 2) * 64:(kv % 2) * 64 + 64, (kv // 2) * T:(kv // 2 + 1) * T]
            p.dma(B2d[0:64, kv * T:(kv + 1) * T], src)
            p.dma(B2d[64:128, kv * T:(kv + 1) * T], src)
        p.bar()

    load_w(p, wv, wbuf[:, 0:ND * nv], D, nv)
    if layer in (0, 1):
        bcast_row(p, _row(p, bvr_d, rowb, D), bb, onesr[:], psA, D)
        lin_tm(p, wbuf[:, 0:ND * nv], B0[:], B3[:], psA, psB, nv, bias_bc=bb[:])
    else:
        lin_tm(p, wbuf[:, 0:ND * nv], B0[:], B3[:], psA, psB, nv)

    scale = 0.125 if layer in (0, 1, 3) else 1.0
    attn = B0

    if layer != 2:
        for h in range(nh):
            kvh = h // 4 if layer == 3 else h
            qp0 = (h % 2) * 64
            qtile = h // 2
            if layer == 3:
                KTs, kp0, ktile = B2d, qp0, kvh
            else:
                KTs, kp0, ktile = B2, qp0, qtile
            for qb in range(4):
                nkt = NT if layer != 3 else 4 * (qb + 1)
                for kt in range(nkt):
                    ps = (psA, psB)[kt % 2]
                    lhsT = KTs[kp0:kp0 + 64,
                               ktile * T + kt * 128: ktile * T + (kt + 1) * 128]
                    rhs = B1[qp0:qp0 + 64,
                             qtile * T + qb * 512: qtile * T + qb * 512 + 512]
                    mm(p, ps[:, 0:512], lhsT, rhs, True, True)
                    p.bar()
                    edst = ebuf[:, kt * 512:(kt + 1) * 512]
                    if layer == 3 and kt >= 4 * qb:
                        di = kt - 4 * qb
                        p.v(lambda e, q=ps, d=di: e.tensor_tensor(
                            sm[:], q[:, 0:512], cmb[:, d * 512:(d + 1) * 512], AL.add))
                        p.bar()
                        p.a(lambda e, d=edst: e.activation(d, sm[:], AF.Exp,
                                                           scale=scale))
                    else:
                        p.a(lambda e, d=edst, q=ps: e.activation(
                            d, q[:, 0:512], AF.Exp, scale=scale))
                    p.bar()
                for kt in range(nkt):
                    mm(p, prow[:, 0:512], onescb[:],
                       ebuf[:, kt * 512:(kt + 1) * 512], kt == 0, kt == nkt - 1)
                p.bar()
                p.a(lambda e, q=qb: e.copy(den[:, q * 512:(q + 1) * 512],
                                           prow[:, 0:512]))
                p.bar()
                for qs in range(4):
                    for kt in range(nkt):
                        mm(p, pout[:, 0:64],
                           ebuf[:, kt * 512 + qs * 128: kt * 512 + (qs + 1) * 128],
                           B3[:, kt * nv + kvh * 64: kt * nv + kvh * 64 + 64],
                           kt == 0, kt == nkt - 1)
                    p.bar()
                    gt = qb * 4 + qs
                    p.a(lambda e, g=gt, hh=h: e.copy(
                        attn[:, g * D + hh * 64: g * D + hh * 64 + 64],
                        pout[:, 0:64]))
                    p.bar()
            for tt in range(NT):
                p.t(lambda e, t2=tt: e.transpose(
                    ptf[0:128, 0:1], den[0:1, t2 * 128:(t2 + 1) * 128],
                    sc["idf"][0:1, 0:1]))
                p.bar()
                p.a(lambda e, t2=tt: e.copy(denc[:, t2:t2 + 1], ptf[0:128, 0:1]))
                p.bar()
            p.v(lambda e: e.reciprocal(rec[:, 0:16], denc[:, 0:16]))
            p.bar()
            for tt in range(NT):
                p.v(lambda e, t2=tt, hh=h: e.tensor_scalar(
                    attn[:, t2 * D + hh * 64: t2 * D + hh * 64 + 64],
                    attn[:, t2 * D + hh * 64: t2 * D + hh * 64 + 64],
                    rec[:, t2:t2 + 1], None, AL.mult))
            p.bar()
    else:
        eq = wbuf[:, 0:T]
        eT = wbuf[:, T:2 * T]
        nmx = nc.alloc_sbuf_tensor("nmx", [128, 8], F32)
        dc8 = nc.alloc_sbuf_tensor("dc8", [128, 8], F32)
        blocks = [psA, psB, psC, pout]
        for h in range(nh):
            qp0 = (h % 2) * 64
            qtile = h // 2
            for qt in range(NT):
                lhsT = B1[qp0:qp0 + 64, qtile * T + qt * 128: qtile * T + (qt + 1) * 128]
                for kb in range(4):
                    mm(p, blocks[kb][:, 0:512], lhsT,
                       B2[qp0:qp0 + 64, qtile * T + kb * 512: qtile * T + kb * 512 + 512],
                       True, True)
                p.bar()
                for kb in range(4):
                    p.v(lambda e, k=kb: e.reduce_max(nmx[:, k:k + 1],
                                                     blocks[k][:, 0:512], axis=AX.X))
                p.bar()
                p.v(lambda e: e.tensor_max(nmx[:, 4:5], nmx[:, 0:1], nmx[:, 1:2]))
                p.v(lambda e: e.tensor_max(nmx[:, 5:6], nmx[:, 2:3], nmx[:, 3:4]))
                p.bar()
                p.v(lambda e: e.tensor_max(nmx[:, 6:7], nmx[:, 4:5], nmx[:, 5:6]))
                p.bar()
                p.v(lambda e: e.tensor_scalar_mul(nmx[:, 7:8], nmx[:, 6:7], -1.0))
                p.bar()
                for kb in range(4):
                    p.a(lambda e, k=kb: e.activation(
                        eq[:, k * 512:(k + 1) * 512], blocks[k][:, 0:512], AF.Exp,
                        bias=nmx[:, 7:8]))
                p.bar()
                for kb in range(4):
                    p.v(lambda e, k=kb: e.reduce_sum(
                        dc8[:, k:k + 1], eq[:, k * 512:(k + 1) * 512], axis=AX.X))
                p.bar()
                p.v(lambda e: e.tensor_add(dc8[:, 4:5], dc8[:, 0:1], dc8[:, 1:2]))
                p.v(lambda e: e.tensor_add(dc8[:, 5:6], dc8[:, 2:3], dc8[:, 3:4]))
                p.bar()
                p.v(lambda e: e.tensor_add(dc8[:, 6:7], dc8[:, 4:5], dc8[:, 5:6]))
                p.bar()
                p.v(lambda e: e.reciprocal(dc8[:, 7:8], dc8[:, 6:7]))
                p.bar()
                items = []
                for kt in range(NT):
                    items.append((eq[:, kt * 128:(kt + 1) * 128],
                                  eT[:, kt * 128:(kt + 1) * 128]))
                transposes(p, items, ptb0, ptb1, sc["idbf"])
                for kt in range(NT):
                    mm(p, pout[:, 0:64], eT[:, kt * 128:(kt + 1) * 128],
                       B3[:, kt * D + h * 64: kt * D + h * 64 + 64],
                       kt == 0, kt == NT - 1)
                p.bar()
                p.v(lambda e, t2=qt, hh=h: e.tensor_scalar(
                    attn[:, t2 * D + hh * 64: t2 * D + hh * 64 + 64],
                    pout[:, 0:64], dc8[:, 7:8], 0.125, AL.mult, AL.mult))
                p.bar()

    if has_o:
        load_w(p, wo, wbuf[:], D, D)
        if layer == 0:
            bcast_row(p, _row(p, bor_d, rowb, D), bb, onesr[:], psA, D)
        attnT = nc.alloc_sbuf_tensor("attnT", [128, ND * 128], BF16)
        for tt in range(NT):
            items = []
            for dc in range(ND):
                items.append((attn[:, tt * D + dc * 128: tt * D + (dc + 1) * 128],
                              attnT[:, dc * 128:(dc + 1) * 128]))
            transposes(p, items, ptb0, ptb1, sc["idbf"])
            p.dma(xb[0][:], xd[tt * 128:(tt + 1) * 128, :])
            p.bar()
            for ob in range(2):
                ps = (psA, psB)[ob]
                for k in range(ND):
                    mm(p, ps[:, 0:512], attnT[:, k * 128:(k + 1) * 128],
                       wbuf[:, k * D + ob * 512: k * D + ob * 512 + 512],
                       k == 0, k == ND - 1)
            p.bar()
            for ob in range(2):
                ps = (psA, psB)[ob]
                if layer == 0:
                    p.v(lambda e, q=ps, o=ob: e.tensor_tensor(
                        sm[:], q[:, 0:512], bb[:, o * 512:(o + 1) * 512], AL.add))
                else:
                    p.v(lambda e, q=ps: e.tensor_copy(sm[:], q[:, 0:512]))
                p.bar()
                p.v(lambda e, o=ob: e.tensor_tensor(
                    xb[0][:, o * 512:(o + 1) * 512],
                    xb[0][:, o * 512:(o + 1) * 512], sm[:], AL.add))
                p.bar()
            p.dma(xo[tt * 128:(tt + 1) * 128, :], xb[0][:])
            p.bar()
    else:
        for tt in range(NT):
            b = xb[tt % 2]
            p.dma(b[:], xd[tt * 128:(tt + 1) * 128, :])
            p.bar()
            p.v(lambda e, t2=tt, bu=b: e.tensor_tensor(
                bu[:], bu[:], attn[:, t2 * D:(t2 + 1) * D], AL.add))
            p.bar()
            p.dma(xo[tt * 128:(tt + 1) * 128, :], b[:])
            p.bar()
    return p.fin()



def build_l0a():
    p = PB()
    nc = p.nc
    xd = nc.dram_tensor("x", [T, D], F32, kind="ExternalInput").ap()
    cos_o = nc.dram_tensor("cos_dist", [T, 8], F32, kind="ExternalOutput").ap()
    pen_o = nc.dram_tensor("pen_part", [1, 1], F32, kind="ExternalOutput").ap()
    wih_d = nc.dram_tensor("wihT", [D, 3072], BF16, kind="ExternalInput").ap()
    whh_d = nc.dram_tensor("whhT", [D, 3072], BF16, kind="ExternalInput").ap()
    expr_d = nc.dram_tensor("exprT", [D, D], BF16, kind="ExternalInput").ap()
    ctx_d = nc.dram_tensor("ctxT", [D, D], BF16, kind="ExternalInput").ap()
    ctxb_d = nc.dram_tensor("ctx_b", [128, 8], F32, kind="ExternalInput").ap()
    ocf_d = nc.dram_tensor("c_ones_col_f", [128, 1], F32, kind="ExternalInput").ap()
    sc = small_scratch(p)
    lnT_scr = nc.dram_tensor("lnT_scr", [ND * 128, T], BF16).ap()

    xwT = nc.alloc_sbuf_tensor("xwT", [128, 24 * T], BF16)
    whh = nc.alloc_sbuf_tensor("whh", [128, 8 * 3072], BF16)
    rout = nc.alloc_sbuf_tensor("rout", [128, 8 * T], BF16)
    xbA = nc.alloc_sbuf_tensor("xbA", [128, D], F32)
    xb = [xbA, xbA]
    h = nc.alloc_sbuf_tensor("h", [128, 8], F32)
    hbf = nc.alloc_sbuf_tensor("hbf", [128, 8], BF16)
    srz = nc.alloc_sbuf_tensor("srz", [128, 16], F32)
    sn1 = nc.alloc_sbuf_tensor("sn1", [128, 8], F32)
    sn2 = nc.alloc_sbuf_tensor("sn2", [128, 8], F32)
    nsg = nc.alloc_sbuf_tensor("nsg", [128, 8], F32)
    nn = nc.alloc_sbuf_tensor("nn", [128, 8], F32)
    za = nc.alloc_sbuf_tensor("za", [128, 8], F32)
    zn = nc.alloc_sbuf_tensor("zn", [128, 8], F32)
    ndd = nc.alloc_sbuf_tensor("ndd", [128, 8], F32)
    mcol = nc.alloc_sbuf_tensor("mcol", [128, 8], BF16)
    onescf = nc.alloc_sbuf_tensor("onescf", [128, 1], F32)

    psA = nc.alloc_psum_tensor("psA", [128, 512], F32)
    psB = nc.alloc_psum_tensor("psB", [128, 512], F32)
    pg = nc.alloc_psum_tensor("pg", [128, 24], F32)
    prow = nc.alloc_psum_tensor("prow", [1, 512], F32)
    ptb0 = nc.alloc_psum_tensor("ptb0", [128, 128], BF16)
    ptb1 = nc.alloc_psum_tensor("ptb1", [128, 128], BF16)
    ptf = nc.alloc_psum_tensor("ptf", [128, 128], F32)

    p.dma(onescf[:], ocf_d)
    p.bar()

    lnT = rout
    emit_ln(p, xd, lnT[:], xb, sc, ptb0, ptb1)

    wb2 = whh
    load_w(p, wih_d, wb2[:, 0:8 * 3072], D, 3072)
    flip = 0
    for gt in range(24):
        for tb in range(4):
            ps = (psA, psB)[flip % 2]
            flip += 1
            for k in range(ND):
                mm(p, ps[:, 0:512],
                   wb2[:, k * 3072 + gt * 128: k * 3072 + (gt + 1) * 128],
                   lnT[:, k * T + tb * 512: k * T + tb * 512 + 512],
                   k == 0, k == ND - 1)
            p.bar()
            p.a(lambda e, g=gt, t2=tb, q=ps: e.copy(
                xwT[:, g * T + t2 * 512: g * T + t2 * 512 + 512], q[:, 0:512]))
            p.bar()

    for k in range(ND):
        p.v(lambda e, kk=k: e.reduce_sum(sc["c0"][:, 0:1],
                                         lnT[:, kk * T:(kk + 1) * T], axis=AX.X))
        p.bar()
        p.v(lambda e, kk=k: e.tensor_scalar_mul(sc["c1"][:, 0:1],
                                                sc["c0"][:, 0:1], 1.0 / T))
        p.bar()
        p.v(lambda e, kk=k: e.tensor_copy(mcol[:, kk:kk + 1], sc["c1"][:, 0:1]))
        p.bar()
    for k in range(ND):
        p.dma(lnT_scr[k * 128:(k + 1) * 128, :], lnT[:, k * T:(k + 1) * T])
    p.bar()
    ctxfull = whh[:, 0:8 * 1024]
    load_w(p, ctx_d, ctxfull, D, D)
    for g in range(8):
        for k in range(ND):
            mm(p, pg[:, g:g + 1],
               ctxfull[:, k * 1024 + g * 128: k * 1024 + (g + 1) * 128],
               mcol[:, k:k + 1], (g == 0 and k == 0), (g == 7 and k == ND - 1))
    p.bar()
    p.dma(srz[:, 0:8], ctxb_d)
    p.bar()
    p.v(lambda e: e.tensor_tensor(h[:], pg[:, 0:8], srz[:, 0:8], AL.add))
    p.bar()
    p.v(lambda e: e.tensor_copy(hbf[:], h[:]))
    p.bar()

    load_w(p, whh_d, whh[:], D, 3072)

    st = nc.alloc_semaphore("st")
    sv = nc.alloc_semaphore("sv")
    sa = nc.alloc_semaphore("sa")
    p.bar()
    for en in ("tensor", "vector", "scalar"):
        p._pre(en)
    nc.vector.sem_inc(sv, 1)
    xwv = xwT[:].rearrange("p (g t) -> p g t", g=24)
    routv = rout[:].rearrange("p (g t) -> p g t", g=8)
    pgv = pg[:].rearrange("p (g one) -> p g one", one=1)
    NV = 10

    with nc.Fori(0, T, engines=[ET.PE, ET.DVE, ET.Activation]) as i:
        nc.tensor.wait_ge(sv, i * NV)
        last = None
        for mt in range(24):
            for k in range(8):
                last = nc.tensor.matmul(
                    pg[:, mt:mt + 1],
                    whh[:, k * 3072 + mt * 128: k * 3072 + (mt + 1) * 128],
                    hbf[:, k:k + 1],
                    start=(mt == 0 and k == 0), stop=(mt == 23 and k == 7),
                    skip_group_check=True)
        last.then_inc(st, 1)
        base = 1 + i * NV

        def g3(ap):
            return ap.rearrange("p (g one) -> p g one", one=1)

        nc.vector.wait_ge(st, i + 1)
        nc.vector.tensor_tensor(g3(srz[:]), g3(pg[:, 0:16]),
                                xwv[:, 0:16, bass.ts(i, 1)],
                                AL.add).then_inc(sv, 1)
        nc.scalar.wait_ge(sv, base + 1)
        nc.scalar.activation(srz[:], srz[:], AF.Sigmoid).then_inc(sa, 1)
        nc.vector.wait_ge(sa, i * 2 + 1)
        nc.vector.tensor_tensor(sn1[:], srz[:, 0:8], pg[:, 16:24],
                                AL.mult).then_inc(sv, 1)
        nc.vector.wait_ge(sv, base + 2)
        nc.vector.tensor_tensor(g3(sn2[:]), g3(sn1[:]),
                                xwv[:, 16:24, bass.ts(i, 1)],
                                AL.add).then_inc(sv, 1)
        nc.scalar.wait_ge(sv, base + 3)
        nc.scalar.activation(nsg[:], sn2[:], AF.Sigmoid,
                             scale=2.0).then_inc(sa, 1)
        nc.vector.wait_ge(sa, i * 2 + 2)
        nc.vector.tensor_scalar(nn[:], nsg[:], 2.0, -1.0, AL.mult,
                                AL.add).then_inc(sv, 1)
        nc.vector.wait_ge(sv, base + 4)
        nc.vector.tensor_tensor(za[:], srz[:, 8:16], h[:], AL.mult).then_inc(sv, 1)
        nc.vector.wait_ge(sv, base + 5)
        nc.vector.tensor_tensor(zn[:], srz[:, 8:16], nn[:], AL.mult).then_inc(sv, 1)
        nc.vector.wait_ge(sv, base + 6)
        nc.vector.tensor_tensor(ndd[:], nn[:], zn[:], AL.subtract).then_inc(sv, 1)
        nc.vector.wait_ge(sv, base + 7)
        nc.vector.tensor_tensor(h[:], ndd[:], za[:], AL.add).then_inc(sv, 1)
        nc.vector.wait_ge(sv, base + 8)
        nc.vector.tensor_copy(hbf[:], h[:]).then_inc(sv, 1)
        nc.vector.wait_ge(sv, base + 9)
        nc.vector.tensor_copy(routv[:, :, bass.ts(i, 1)],
                              g3(hbf[:])).then_inc(sv, 1)

    for en in ("tensor", "vector", "scalar", "sync", "gpsimd"):
        e = getattr(nc, en)
        e.wait_ge(st, T)
        e.wait_ge(sv, 1 + NV * T)
        e.wait_ge(sa, 2 * T)

    rtm = xwT
    lnT2o = 16 * 1024
    exo = 32 * 1024
    for k in range(8):
        items = []
        for tt in range(NT):
            items.append((rout[:, k * T + tt * 128: k * T + (tt + 1) * 128],
                          rtm[:, tt * D + k * 128: tt * D + (k + 1) * 128]))
        transposes(p, items, ptb0, ptb1, sc["idbf"])
    for k in range(ND):
        p.dma(rtm[:, lnT2o + k * T:(lnT2o) + (k + 1) * T],
              lnT_scr[k * 128:(k + 1) * 128, :])
    p.bar()
    exw = nc.alloc_sbuf_tensor("exw", [128, ND * D], BF16)
    load_w(p, expr_d, exw[:], D, D)
    lin_tm(p, exw[:], rtm[:, lnT2o:lnT2o + ND * T], rtm[:, exo:exo + NT * D],
           psA, psB, D)

    gram = nc.alloc_sbuf_tensor("gram", [128, 64], F32)
    gsc = nc.alloc_sbuf_tensor("gsc", [128, 128], BF16)
    ncol = nc.alloc_sbuf_tensor("ncol", [128, 8], F32)
    ecol = nc.alloc_sbuf_tensor("ecol", [128, 8], F32)
    numc = nc.alloc_sbuf_tensor("numc", [128, 8], F32)
    cosb = nc.alloc_sbuf_tensor("cosdb", [128, 8], F32)
    penacc = nc.alloc_sbuf_tensor("penacc", [128, 1], F32)
    p.g(lambda e: e.memset(penacc[:], 0.0))
    p.bar()
    for tt in range(NT):
        rt = rtm[:, tt * D:(tt + 1) * D]
        ex = rtm[:, exo + tt * D: exo + (tt + 1) * D]
        for e1 in range(8):
            p.v(lambda e, a=e1, r=rt: e.tensor_mul(
                gsc[:], r[:, a * 128:(a + 1) * 128], r[:, a * 128:(a + 1) * 128]))
            p.bar()
            p.v(lambda e, a=e1: e.reduce_sum(ncol[:, a:a + 1], gsc[:], axis=AX.X))
            p.bar()
        p.a(lambda e: e.activation(ncol[:], ncol[:], AF.Sqrt))
        p.bar()
        p.v(lambda e: e.tensor_scalar_max(ncol[:], ncol[:], 1e-12))
        p.bar()
        p.v(lambda e: e.reciprocal(ecol[:], ncol[:]))
        p.bar()
        for e1 in range(8):
            p.v(lambda e, a=e1, r=rt: e.tensor_scalar(
                r[:, a * 128:(a + 1) * 128], r[:, a * 128:(a + 1) * 128],
                ecol[:, a:a + 1], None, AL.mult))
        p.bar()
        for e1 in range(8):
            for f1 in range(e1, 8):
                p.v(lambda e, a=e1, b2=f1, r=rt: e.tensor_mul(
                    gsc[:], r[:, a * 128:(a + 1) * 128], r[:, b2 * 128:(b2 + 1) * 128]))
                p.bar()
                p.v(lambda e, a=e1, b2=f1: e.reduce_sum(
                    gram[:, a * 8 + b2: a * 8 + b2 + 1], gsc[:], axis=AX.X))
                p.bar()
                if f1 != e1:
                    p.v(lambda e, a=e1, b2=f1: e.tensor_copy(
                        gram[:, b2 * 8 + a: b2 * 8 + a + 1],
                        gram[:, a * 8 + b2: a * 8 + b2 + 1]))
                    p.bar()
        for e1 in range(8):
            p.v(lambda e, a=e1: e.tensor_copy(ncol[:, a:a + 1],
                                              gram[:, a * 8 + a: a * 8 + a + 1]))
        p.bar()
        p.a(lambda e: e.activation(ncol[:], ncol[:], AF.Sqrt))
        p.bar()
        for e1 in range(8):
            p.v(lambda e, a=e1, r=rt, x2=ex: e.tensor_mul(
                gsc[:], r[:, a * 128:(a + 1) * 128], x2[:, a * 128:(a + 1) * 128]))
            p.bar()
            p.v(lambda e, a=e1: e.reduce_sum(numc[:, a:a + 1], gsc[:], axis=AX.X))
            p.bar()
            p.v(lambda e, a=e1, x2=ex: e.tensor_mul(
                gsc[:], x2[:, a * 128:(a + 1) * 128], x2[:, a * 128:(a + 1) * 128]))
            p.bar()
            p.v(lambda e, a=e1: e.reduce_sum(ecol[:, a:a + 1], gsc[:], axis=AX.X))
            p.bar()
        p.a(lambda e: e.activation(ecol[:], ecol[:], AF.Sqrt))
        p.bar()
        for e1 in range(8):
            p.v(lambda e, a=e1: e.tensor_scalar_add(
                gram[:, a * 9:a * 9 + 1], gram[:, a * 9:a * 9 + 1], -1.0))
        p.bar()
        p.v(lambda e: e.tensor_mul(gram[:], gram[:], gram[:]))
        p.bar()
        for e1 in range(8):
            p.v(lambda e, a=e1: e.reduce_sum(cosb[:, a:a + 1],
                                             gram[:, a * 8:(a + 1) * 8], axis=AX.X))
        p.bar()
        p.v(lambda e: e.tensor_scalar_max(gram[:, 0:8], cosb[:, 0:8], 1e-24))
        p.bar()
        p.v(lambda e: e.reciprocal(gram[:, 8:16], gram[:, 0:8]))
        p.bar()
        p.v(lambda e: e.tensor_mul(gram[:, 16:24], cosb[:, 0:8], gram[:, 8:16]))
        p.bar()
        p.v(lambda e: e.reduce_sum(gram[:, 24:25], gram[:, 16:24], axis=AX.X))
        p.bar()
        p.v(lambda e: e.tensor_add(penacc[:], penacc[:], gram[:, 24:25]))
        p.bar()
        p.v(lambda e: e.tensor_scalar_max(ncol[:], ncol[:], 1e-8))
        p.v(lambda e: e.tensor_scalar_max(ecol[:], ecol[:], 1e-8))
        p.bar()
        p.v(lambda e: e.tensor_mul(ncol[:], ncol[:], ecol[:]))
        p.bar()
        p.v(lambda e: e.reciprocal(ncol[:], ncol[:]))
        p.bar()
        p.v(lambda e: e.tensor_mul(numc[:], numc[:], ncol[:]))
        p.bar()
        p.v(lambda e: e.tensor_scalar(cosb[:], numc[:], -1.0, 1.0, AL.mult, AL.add))
        p.bar()
        p.dma(cos_o[tt * 128:(tt + 1) * 128, :], cosb[:])
        p.bar()
    mm(p, prow[:, 0:1], onescf[:], penacc[:], True, True)
    p.bar()
    p.a(lambda e: e.copy(sc["c0"][0:1, 0:1], prow[:, 0:1]))
    p.bar()
    p.dma(pen_o[0:1, 0:1], sc["c0"][0:1, 0:1])
    return p.fin()



def build_l0b():
    p = PB()
    nc = p.nc
    xd = nc.dram_tensor("x", [T, D], F32, kind="ExternalInput").ap()
    xo = nc.dram_tensor("x_out", [T, D], F32, kind="ExternalOutput").ap()
    cos_d = nc.dram_tensor("cos_dist", [T, 8], F32, kind="ExternalInput").ap()
    pen_d = nc.dram_tensor("pens", [1, 8], F32, kind="ExternalInput").ap()
    gws = [nc.dram_tensor(f"gw{e}", [D, 2048], BF16, kind="ExternalInput").ap()
           for e in range(8)]
    uws = [nc.dram_tensor(f"uw{e}", [D, 2048], BF16, kind="ExternalInput").ap()
           for e in range(8)]
    dws = [nc.dram_tensor(f"dw{e}", [2048, D], BF16, kind="ExternalInput").ap()
           for e in range(8)]
    sgw = nc.dram_tensor("sgw", [D, 2048], BF16, kind="ExternalInput").ap()
    suw = nc.dram_tensor("suw", [D, 2048], BF16, kind="ExternalInput").ap()
    sdw = nc.dram_tensor("sdw", [2048, D], BF16, kind="ExternalInput").ap()
    sc = small_scratch(p)

    lnT = nc.alloc_sbuf_tensor("lnT", [128, ND * T], BF16)
    actT = nc.alloc_sbuf_tensor("actT", [128, 16 * T], BF16)
    accb = nc.alloc_sbuf_tensor("accb", [128, NT * D], BF16)
    wg = nc.alloc_sbuf_tensor("wg", [128, ND * 2048], BF16)
    cw = nc.alloc_sbuf_tensor("cw", [128, NT * 8], F32)
    xb = [nc.alloc_sbuf_tensor("xbA", [128, D], F32),
          nc.alloc_sbuf_tensor("xbB", [128, D], F32)]
    m8 = nc.alloc_sbuf_tensor("m8", [128, 16], F32)
    sm = nc.alloc_sbuf_tensor("sm", [128, 512], F32)

    psA = nc.alloc_psum_tensor("psA", [128, 512], F32)
    psB = nc.alloc_psum_tensor("psB", [128, 512], F32)
    psC = nc.alloc_psum_tensor("psC", [128, 512], F32)
    psD = nc.alloc_psum_tensor("psD", [128, 512], F32)
    ptb0 = nc.alloc_psum_tensor("ptb0", [128, 128], BF16)
    ptb1 = nc.alloc_psum_tensor("ptb1", [128, 128], BF16)

    onep = nc.alloc_sbuf_tensor("onep", [128, 1], F32)
    orow_d = nc.dram_tensor("c_ones_row_f", [1, 128], F32, kind="ExternalInput").ap()
    onesr = nc.alloc_sbuf_tensor("onesr", [1, 128], F32)
    p.dma(m8[0:1, 0:8], pen_d)
    p.dma(onesr[:], orow_d)
    p.bar()
    p.v(lambda e: e.reduce_sum(m8[0:1, 8:9], m8[0:1, 0:8], axis=AX.X))
    p.bar()
    p.v(lambda e: e.tensor_scalar(m8[0:1, 9:10], m8[0:1, 8:9],
                                  1.0 / (NCORES * T), 1.0, AL.mult, AL.add))
    p.bar()
    mm(p, psA[:, 0:1], onesr[:], m8[0:1, 9:10], True, True)
    p.bar()
    p.a(lambda e: e.copy(onep[:], psA[:, 0:1]))
    p.bar()

    for tt in range(NT):
        p.dma(m8[:, 0:8], cos_d[tt * 128:(tt + 1) * 128, :])
        p.bar()
        p.v(lambda e: e.tensor_scalar(m8[:, 0:8], m8[:, 0:8], onep[:, 0:1],
                                      None, AL.mult))
        p.bar()
        p.v(lambda e: e.reduce_max(m8[:, 8:9], m8[:, 0:8], axis=AX.X))
        p.bar()
        oh1 = sm[:, 8:16]
        p.v(lambda e: e.tensor_scalar(oh1, m8[:, 0:8], m8[:, 8:9], None,
                                      AL.is_equal))
        p.bar()
        p.v(lambda e: e.scalar_tensor_tensor(sm[:, 16:24], oh1, NEG, m8[:, 0:8],
                                             AL.mult, AL.add))
        p.bar()
        p.v(lambda e: e.reduce_max(m8[:, 9:10], sm[:, 16:24], axis=AX.X))
        p.bar()
        oh2 = sm[:, 24:32]
        p.v(lambda e: e.tensor_scalar(oh2, sm[:, 16:24], m8[:, 9:10], None,
                                      AL.is_equal))
        p.bar()
        p.v(lambda e: e.tensor_tensor(m8[:, 10:11], m8[:, 8:9], m8[:, 9:10],
                                      AL.subtract))
        p.bar()
        p.a(lambda e: e.activation(m8[:, 11:12], m8[:, 10:11], AF.Sigmoid))
        p.bar()
        p.v(lambda e: e.tensor_scalar(m8[:, 12:13], m8[:, 11:12], -1.0, 1.0,
                                      AL.mult, AL.add))
        p.bar()
        p.v(lambda e, t2=tt: e.tensor_scalar(cw[:, t2 * 8:(t2 + 1) * 8], oh1,
                                             m8[:, 11:12], None, AL.mult))
        p.bar()
        p.v(lambda e, t2=tt: e.scalar_tensor_tensor(
            cw[:, t2 * 8:(t2 + 1) * 8], oh2, m8[:, 12:13],
            cw[:, t2 * 8:(t2 + 1) * 8], AL.mult, AL.add))
        p.bar()

    emit_ln(p, xd, lnT[:], xb, sc, ptb0, ptb1)
    p.g(lambda e: e.memset(accb[:], 0.0))
    p.bar()

    def expert(gw_d, uw_d, dw_d, weight_col):
        load_w(p, gw_d, wg[:], D, 2048)
        flip = 0
        for ft in range(16):
            for tb in range(4):
                pg_ = (psA, psB)[flip % 2]
                flip += 1
                for k in range(ND):
                    mm(p, pg_[:, 0:512],
                       wg[:, k * 2048 + ft * 128: k * 2048 + (ft + 1) * 128],
                       lnT[:, k * T + tb * 512: k * T + tb * 512 + 512],
                       k == 0, k == ND - 1)
                p.bar()
                p.a(lambda e, q=pg_, f=ft, t2=tb: e.activation(
                    actT[:, f * T + t2 * 512: f * T + t2 * 512 + 512],
                    q[:, 0:512], AF.Silu))
                p.bar()
        load_w(p, uw_d, wg[:], D, 2048)
        flip = 0
        for ft in range(16):
            for tb in range(4):
                pu_ = (psC, psD)[flip % 2]
                flip += 1
                for k in range(ND):
                    mm(p, pu_[:, 0:512],
                       wg[:, k * 2048 + ft * 128: k * 2048 + (ft + 1) * 128],
                       lnT[:, k * T + tb * 512: k * T + tb * 512 + 512],
                       k == 0, k == ND - 1)
                p.bar()
                dst = actT[:, ft * T + tb * 512: ft * T + tb * 512 + 512]
                p.v(lambda e, q=pu_, d=dst: e.tensor_tensor(d, d, q[:, 0:512],
                                                            AL.mult))
                p.bar()
        load_w(p, dw_d, wg[:], 2048, D)
        flip = 0
        for tt in range(NT):
            for ob in range(2):
                ps = (psA, psB)[flip % 2]
                flip += 1
                for k in range(16):
                    mm(p, ps[:, 0:512],
                       actT[:, k * T + tt * 128: k * T + (tt + 1) * 128],
                       wg[:, k * D + ob * 512: k * D + ob * 512 + 512],
                       k == 0, k == 15)
                p.bar()
                dst = accb[:, tt * D + ob * 512: tt * D + ob * 512 + 512]
                if weight_col is None:
                    p.v(lambda e, d=dst, q=ps: e.tensor_tensor(
                        d, q[:, 0:512], d, AL.add))
                else:
                    wc = weight_col(tt)
                    p.v(lambda e, d=dst, q=ps, w2=wc: e.scalar_tensor_tensor(
                        d, q[:, 0:512], w2, d, AL.mult, AL.add))
                p.bar()

    for e1 in range(8):
        expert(gws[e1], uws[e1], dws[e1],
               lambda tt, ee=e1: cw[:, tt * 8 + ee: tt * 8 + ee + 1])
    expert(sgw, suw, sdw, None)

    for tt in range(NT):
        b = xb[tt % 2]
        p.dma(b[:], xd[tt * 128:(tt + 1) * 128, :])
        p.bar()
        p.v(lambda e, t2=tt, bu=b: e.scalar_tensor_tensor(
            bu[:], accb[:, t2 * D:(t2 + 1) * D], 0.5, bu[:], AL.mult, AL.add))
        p.bar()
        p.dma(xo[tt * 128:(tt + 1) * 128, :], b[:])
        p.bar()
    return p.fin()


def build_ff1():
    p = PB()
    nc = p.nc
    xd = nc.dram_tensor("x", [T, D], F32, kind="ExternalInput").ap()
    xo = nc.dram_tensor("x_out", [T, D], F32, kind="ExternalOutput").ap()
    wup_d = nc.dram_tensor("wupq", [D, 4096], BF16, kind="ExternalInput").ap()
    wdn_d = nc.dram_tensor("wdnq", [4096, D], BF16, kind="ExternalInput").ap()
    orow_d = nc.dram_tensor("c_ones_row_f", [1, 128], F32, kind="ExternalInput").ap()
    ocb_d = nc.dram_tensor("c_ones_col_bf", [128, 1], BF16, kind="ExternalInput").ap()
    sc = small_scratch(p)
    h_scr = nc.dram_tensor("h_scr", [4096, T], BF16).ap()

    xnT = nc.alloc_sbuf_tensor("xnT", [128, ND * T], BF16)
    wbig = nc.alloc_sbuf_tensor("wbig", [128, ND * 4096], BF16)
    inv2b = nc.alloc_sbuf_tensor("inv2b", [128, T], F32)
    i2row = nc.alloc_sbuf_tensor("i2row", [1, T], F32)
    xb = [nc.alloc_sbuf_tensor("xbA", [128, D], F32),
          nc.alloc_sbuf_tensor("xbB", [128, D], F32)]
    onesr = nc.alloc_sbuf_tensor("onesr", [1, 128], F32)
    onescb = nc.alloc_sbuf_tensor("onescb", [128, 1], BF16)
    hb = nc.alloc_sbuf_tensor("hb", [128, 512], BF16)
    sq = nc.alloc_sbuf_tensor("sq", [128, 512], BF16)

    psA = nc.alloc_psum_tensor("psA", [128, 512], F32)
    psB = nc.alloc_psum_tensor("psB", [128, 512], F32)
    prow = nc.alloc_psum_tensor("prow", [1, 512], F32)
    pbc = nc.alloc_psum_tensor("pbc", [128, 512], F32)
    ptb0 = nc.alloc_psum_tensor("ptb0", [128, 128], BF16)
    ptb1 = nc.alloc_psum_tensor("ptb1", [128, 128], BF16)

    p.dma(onesr[:], orow_d)
    p.dma(onescb[:], ocb_d)
    p.bar()
    emit_ln(p, xd, xnT[:], xb, sc, ptb0, ptb1, rms_extra=True)
    load_w(p, wup_d, wbig[:], D, 4096)

    for tb in range(4):
        flip = 0
        for ft in range(32):
            ps = (psA, psB)[flip % 2]
            flip += 1
            for k in range(ND):
                mm(p, ps[:, 0:512],
                   wbig[:, k * 4096 + ft * 128: k * 4096 + (ft + 1) * 128],
                   xnT[:, k * T + tb * 512: k * T + tb * 512 + 512],
                   k == 0, k == ND - 1)
            p.bar()
            p.a(lambda e, q=ps: e.activation(hb[:], q[:, 0:512], AF.Gelu))
            p.bar()
            p.dma(h_scr[ft * 128:(ft + 1) * 128, tb * 512:(tb + 1) * 512], hb[:])
            p.a(lambda e: e.activation(sq[:], hb[:], AF.Square))
            p.bar()
            mm(p, prow[:, 0:512], onescb[:], sq[:], ft == 0, ft == 31)
            p.bar()
        p.a(lambda e, t2=tb: e.activation(i2row[0:1, t2 * 512:(t2 + 1) * 512],
                                          prow[:, 0:512], AF.Sqrt,
                                          scale=1.0 / 4096.0))
        p.bar()
        p.v(lambda e, t2=tb: e.tensor_scalar_add(
            i2row[0:1, t2 * 512:(t2 + 1) * 512],
            i2row[0:1, t2 * 512:(t2 + 1) * 512], 1e-8))
        p.bar()
        p.v(lambda e, t2=tb: e.reciprocal(i2row[0:1, t2 * 512:(t2 + 1) * 512],
                                          i2row[0:1, t2 * 512:(t2 + 1) * 512]))
        p.bar()
        mm(p, pbc[:, 0:512], onesr[:], i2row[0:1, tb * 512:(tb + 1) * 512],
           True, True)
        p.bar()
        p.a(lambda e, t2=tb: e.copy(inv2b[:, t2 * 512:(t2 + 1) * 512],
                                    pbc[:, 0:512]))
        p.bar()

    load_w(p, wdn_d, wbig[:, 0:32 * D], 4096, D)
    hn = nc.alloc_sbuf_tensor("hn", [128, 32 * 128], BF16)
    for tt in range(NT):
        for k in range(32):
            p.dma(hn[:, k * 128:(k + 1) * 128],
                  h_scr[k * 128:(k + 1) * 128, tt * 128:(tt + 1) * 128])
        p.bar()
        for k in range(32):
            p.v(lambda e, kk=k, t2=tt: e.tensor_tensor(
                hn[:, kk * 128:(kk + 1) * 128], hn[:, kk * 128:(kk + 1) * 128],
                inv2b[:, t2 * 128:(t2 + 1) * 128], AL.mult))
        p.bar()
        for ob in range(2):
            ps = (psA, psB)[ob]
            for k in range(32):
                mm(p, ps[:, 0:512], hn[:, k * 128:(k + 1) * 128],
                   wbig[:, k * D + ob * 512: k * D + ob * 512 + 512],
                   k == 0, k == 31)
        p.bar()
        b = xb[tt % 2]
        p.dma(b[:], xd[tt * 128:(tt + 1) * 128, :])
        p.bar()
        for ob in range(2):
            ps = (psA, psB)[ob]
            p.v(lambda e, q=ps, o=ob, bu=b: e.scalar_tensor_tensor(
                bu[:, o * 512:(o + 1) * 512], q[:, 0:512], 0.5,
                bu[:, o * 512:(o + 1) * 512], AL.mult, AL.add))
        p.bar()
        p.dma(xo[tt * 128:(tt + 1) * 128, :], b[:])
        p.bar()
    return p.fin()


def build_ff2():
    p = PB()
    nc = p.nc
    xd = nc.dram_tensor("x", [T, D], F32, kind="ExternalInput").ap()
    xo = nc.dram_tensor("x_out", [T, D], F32, kind="ExternalOutput").ap()
    wr_d = nc.dram_tensor("wrT", [D, 8], BF16, kind="ExternalInput").ap()
    fcs = [nc.dram_tensor(f"fc{e}", [D, 512], BF16, kind="ExternalInput").ap()
           for e in range(8)]
    pjs = [nc.dram_tensor(f"pj{e}", [512, D], BF16, kind="ExternalInput").ap()
           for e in range(8)]
    sc = small_scratch(p)

    lnT = nc.alloc_sbuf_tensor("lnT", [128, ND * T], BF16)
    hT = nc.alloc_sbuf_tensor("hT", [128, 4 * T], BF16)
    accb = nc.alloc_sbuf_tensor("accb", [128, NT * D], BF16)
    wsb = nc.alloc_sbuf_tensor("wsb", [128, ND * 512], BF16)
    oh = nc.alloc_sbuf_tensor("oh", [128, NT * 8], F32)
    m8 = nc.alloc_sbuf_tensor("m8", [128, 16], F32)
    sm = nc.alloc_sbuf_tensor("sm", [128, 512], F32)
    sm2 = nc.alloc_sbuf_tensor("sm2", [128, 512], F32)
    xb = [nc.alloc_sbuf_tensor("xbA", [128, D], F32),
          nc.alloc_sbuf_tensor("xbB", [128, D], F32)]

    psA = nc.alloc_psum_tensor("psA", [128, 512], F32)
    psB = nc.alloc_psum_tensor("psB", [128, 512], F32)
    ptb0 = nc.alloc_psum_tensor("ptb0", [128, 128], BF16)
    ptb1 = nc.alloc_psum_tensor("ptb1", [128, 128], BF16)

    emit_ln(p, xd, lnT[:], xb, sc, ptb0, ptb1)
    wr_sb = nc.alloc_sbuf_tensor("wr_sb", [128, ND * 8], BF16)
    load_w(p, wr_d, wr_sb[:], D, 8)
    for tt in range(NT):
        for k in range(ND):
            mm(p, psA[:, 0:8],
               lnT[:, k * T + tt * 128: k * T + (tt + 1) * 128],
               wr_sb[:, k * 8:(k + 1) * 8], k == 0, k == ND - 1)
        p.bar()
        p.v(lambda e: e.reduce_max(m8[:, 0:1], psA[:, 0:8], axis=AX.X))
        p.bar()
        p.v(lambda e, t2=tt: e.tensor_scalar(oh[:, t2 * 8:(t2 + 1) * 8],
                                             psA[:, 0:8], m8[:, 0:1], None,
                                             AL.is_equal))
        p.bar()
    p.g(lambda e: e.memset(accb[:], 0.0))
    p.bar()
    for e1 in range(8):
        load_w(p, fcs[e1], wsb[:], D, 512)
        flip = 0
        for ft in range(4):
            for tb in range(4):
                ps = (psA, psB)[flip % 2]
                flip += 1
                for k in range(ND):
                    mm(p, ps[:, 0:512],
                       wsb[:, k * 512 + ft * 128: k * 512 + (ft + 1) * 128],
                       lnT[:, k * T + tb * 512: k * T + tb * 512 + 512],
                       k == 0, k == ND - 1)
                p.bar()
                p.v(lambda e, q=ps: e.tensor_scalar_max(sm[:], q[:, 0:512], 0.0))
                p.bar()
                p.v(lambda e, q=ps: e.tensor_scalar_min(sm2[:], q[:, 0:512], 0.0))
                p.bar()
                p.v(lambda e: e.scalar_tensor_tensor(sm[:], sm2[:], 0.5, sm[:],
                                                     AL.mult, AL.add))
                p.bar()
                p.a(lambda e, f=ft, t2=tb: e.activation(
                    hT[:, f * T + t2 * 512: f * T + t2 * 512 + 512], sm[:],
                    AF.Square))
                p.bar()
        load_w(p, pjs[e1], wsb[:, 0:4 * D], 512, D)
        flip = 0
        for tt in range(NT):
            for ob in range(2):
                ps = (psA, psB)[flip % 2]
                flip += 1
                for k in range(4):
                    mm(p, ps[:, 0:512],
                       hT[:, k * T + tt * 128: k * T + (tt + 1) * 128],
                       wsb[:, k * D + ob * 512: k * D + ob * 512 + 512],
                       k == 0, k == 3)
                p.bar()
                dst = accb[:, tt * D + ob * 512: tt * D + ob * 512 + 512]
                p.v(lambda e, d=dst, q=ps, t2=tt, ee=e1: e.scalar_tensor_tensor(
                    d, q[:, 0:512], oh[:, t2 * 8 + ee: t2 * 8 + ee + 1], d,
                    AL.mult, AL.add))
                p.bar()
    for tt in range(NT):
        b = xb[tt % 2]
        p.dma(b[:], xd[tt * 128:(tt + 1) * 128, :])
        p.bar()
        p.v(lambda e, t2=tt, bu=b: e.scalar_tensor_tensor(
            bu[:], accb[:, t2 * D:(t2 + 1) * D], 0.5, bu[:], AL.mult, AL.add))
        p.bar()
        p.dma(xo[tt * 128:(tt + 1) * 128, :], b[:])
        p.bar()
    return p.fin()


def build_ff3():
    p = PB()
    nc = p.nc
    xd = nc.dram_tensor("x", [T, D], F32, kind="ExternalInput").ap()
    xo = nc.dram_tensor("x_out", [T, D], F32, kind="ExternalOutput").ap()
    gw_d = nc.dram_tensor("gwT", [D, 8], BF16, kind="ExternalInput").ap()
    ups = [nc.dram_tensor(f"up{e}", [D, D], BF16, kind="ExternalInput").ap()
           for e in range(8)]
    dns = [nc.dram_tensor(f"dn{e}", [D, D], BF16, kind="ExternalInput").ap()
           for e in range(8)]
    upb_d = nc.dram_tensor("upb", [128, 64], F32, kind="ExternalInput").ap()
    dnb_d = nc.dram_tensor("dnb", [8, D], F32, kind="ExternalInput").ap()
    su_d = nc.dram_tensor("suw", [D, D], BF16, kind="ExternalInput").ap()
    sd_d = nc.dram_tensor("sdw", [D, D], BF16, kind="ExternalInput").ap()
    sub_d = nc.dram_tensor("sub", [128, 8], F32, kind="ExternalInput").ap()
    sdb_d = nc.dram_tensor("sdb", [1, D], F32, kind="ExternalInput").ap()
    orow_d = nc.dram_tensor("c_ones_row_f", [1, 128], F32, kind="ExternalInput").ap()
    sc = small_scratch(p)

    lnT = nc.alloc_sbuf_tensor("lnT", [128, ND * T], BF16)
    hT = nc.alloc_sbuf_tensor("hT", [128, ND * T], BF16)
    accb = nc.alloc_sbuf_tensor("accb", [128, NT * D], BF16)
    wsb = nc.alloc_sbuf_tensor("wsb", [128, ND * D], BF16)
    comb = nc.alloc_sbuf_tensor("comb", [128, NT * 8], F32)
    upbc = nc.alloc_sbuf_tensor("upbc", [128, 64], F32)
    dnbc = nc.alloc_sbuf_tensor("dnbc", [128, D], F32)
    rowb = nc.alloc_sbuf_tensor("rowb", [1, D], F32)
    onesr = nc.alloc_sbuf_tensor("onesr", [1, 128], F32)
    m8 = nc.alloc_sbuf_tensor("m8", [128, 16], F32)
    sm = nc.alloc_sbuf_tensor("sm", [128, 512], F32)
    xb = [nc.alloc_sbuf_tensor("xbA", [128, D], F32),
          nc.alloc_sbuf_tensor("xbB", [128, D], F32)]

    psA = nc.alloc_psum_tensor("psA", [128, 512], F32)
    psB = nc.alloc_psum_tensor("psB", [128, 512], F32)
    ptb0 = nc.alloc_psum_tensor("ptb0", [128, 128], BF16)
    ptb1 = nc.alloc_psum_tensor("ptb1", [128, 128], BF16)

    p.dma(onesr[:], orow_d)
    p.dma(upbc[:], upb_d)
    p.bar()
    emit_ln(p, xd, lnT[:], xb, sc, ptb0, ptb1)

    gw_sb = nc.alloc_sbuf_tensor("gw_sb", [128, ND * 8], BF16)
    load_w(p, gw_d, gw_sb[:], D, 8)
    for tt in range(NT):
        for k in range(ND):
            mm(p, psA[:, 0:8],
               lnT[:, k * T + tt * 128: k * T + (tt + 1) * 128],
               gw_sb[:, k * 8:(k + 1) * 8], k == 0, k == ND - 1)
        p.bar()
        p.v(lambda e: e.reduce_max(m8[:, 0:1], psA[:, 0:8], axis=AX.X))
        p.bar()
        p.v(lambda e: e.tensor_scalar_mul(m8[:, 1:2], m8[:, 0:1], -1.0))
        p.bar()
        p.a(lambda e: e.activation(sm[:, 0:8], psA[:, 0:8], AF.Exp,
                                   bias=m8[:, 1:2]))
        p.bar()
        p.v(lambda e: e.reduce_sum(m8[:, 2:3], sm[:, 0:8], axis=AX.X))
        p.bar()
        p.v(lambda e: e.reciprocal(m8[:, 3:4], m8[:, 2:3]))
        p.bar()
        p.v(lambda e: e.tensor_scalar(sm[:, 0:8], sm[:, 0:8], m8[:, 3:4], None,
                                      AL.mult))
        p.bar()
        p.v(lambda e: e.reduce_max(m8[:, 4:5], sm[:, 0:8], axis=AX.X))
        p.bar()
        p.v(lambda e: e.tensor_scalar(sm[:, 8:16], sm[:, 0:8], m8[:, 4:5], None,
                                      AL.is_equal))
        p.bar()
        p.v(lambda e: e.scalar_tensor_tensor(sm[:, 16:24], sm[:, 8:16], NEG,
                                             sm[:, 0:8], AL.mult, AL.add))
        p.bar()
        p.v(lambda e: e.reduce_max(m8[:, 5:6], sm[:, 16:24], axis=AX.X))
        p.bar()
        p.v(lambda e: e.tensor_scalar(sm[:, 24:32], sm[:, 16:24], m8[:, 5:6],
                                      None, AL.is_equal))
        p.bar()
        p.v(lambda e: e.tensor_tensor(m8[:, 6:7], m8[:, 4:5], m8[:, 5:6], AL.add))
        p.bar()
        p.v(lambda e: e.tensor_scalar_add(m8[:, 6:7], m8[:, 6:7], 1e-20))
        p.bar()
        p.v(lambda e: e.reciprocal(m8[:, 7:8], m8[:, 6:7]))
        p.bar()
        p.v(lambda e: e.tensor_mul(m8[:, 8:9], m8[:, 4:5], m8[:, 7:8]))
        p.v(lambda e: e.tensor_mul(m8[:, 9:10], m8[:, 5:6], m8[:, 7:8]))
        p.bar()
        p.v(lambda e, t2=tt: e.tensor_scalar(comb[:, t2 * 8:(t2 + 1) * 8],
                                             sm[:, 8:16], m8[:, 8:9], None,
                                             AL.mult))
        p.bar()
        p.v(lambda e, t2=tt: e.scalar_tensor_tensor(
            comb[:, t2 * 8:(t2 + 1) * 8], sm[:, 24:32], m8[:, 9:10],
            comb[:, t2 * 8:(t2 + 1) * 8], AL.mult, AL.add))
        p.bar()

    p.g(lambda e: e.memset(accb[:], 0.0))
    p.bar()

    def mlp3(up_d, dn_d, ub_col, db_row_idx, weight_col, shared_bias=None):
        load_w(p, up_d, wsb[:], D, D)
        flip = 0
        for ft in range(ND):
            for tb in range(4):
                ps = (psA, psB)[flip % 2]
                flip += 1
                for k in range(ND):
                    mm(p, ps[:, 0:512],
                       wsb[:, k * D + ft * 128: k * D + (ft + 1) * 128],
                       lnT[:, k * T + tb * 512: k * T + tb * 512 + 512],
                       k == 0, k == ND - 1)
                p.bar()
                p.a(lambda e, q=ps, f=ft, t2=tb, b2=ub_col: e.activation(
                    hT[:, f * T + t2 * 512: f * T + t2 * 512 + 512],
                    q[:, 0:512], AF.Silu, bias=b2[:, f:f + 1]))
                p.bar()
        load_w(p, dn_d, wsb[:], D, D)
        if db_row_idx is not None:
            p.dma(rowb[0:1, :], dnb_d[db_row_idx:db_row_idx + 1, :])
            p.bar()
            bcast_row(p, rowb[0:1, :], dnbc[:], onesr[:], psA, D)
        elif shared_bias is not None:
            bcast_row(p, _row(p, shared_bias, rowb, D), dnbc[:], onesr[:], psA, D)
        flip = 0
        for tt in range(NT):
            for ob in range(2):
                ps = (psA, psB)[flip % 2]
                flip += 1
                for k in range(ND):
                    mm(p, ps[:, 0:512],
                       hT[:, k * T + tt * 128: k * T + (tt + 1) * 128],
                       wsb[:, k * D + ob * 512: k * D + ob * 512 + 512],
                       k == 0, k == ND - 1)
                p.bar()
                p.v(lambda e, q=ps, o=ob: e.tensor_tensor(
                    sm[:], q[:, 0:512], dnbc[:, o * 512:(o + 1) * 512], AL.add))
                p.bar()
                dst = accb[:, tt * D + ob * 512: tt * D + ob * 512 + 512]
                if weight_col is None:
                    p.v(lambda e, d=dst: e.tensor_tensor(d, sm[:], d, AL.add))
                else:
                    wc = weight_col(tt)
                    p.v(lambda e, d=dst, w2=wc: e.scalar_tensor_tensor(
                        d, sm[:], w2, d, AL.mult, AL.add))
                p.bar()

    for e1 in range(8):
        mlp3(ups[e1], dns[e1], upbc[:, e1 * 8:(e1 + 1) * 8], e1,
             lambda tt, ee=e1: comb[:, tt * 8 + ee: tt * 8 + ee + 1])
    subc = nc.alloc_sbuf_tensor("subc", [128, 8], F32)
    p.dma(subc[:], sub_d)
    p.bar()
    mlp3(su_d, sd_d, subc, None, None, shared_bias=sdb_d)

    for tt in range(NT):
        b = xb[tt % 2]
        p.dma(b[:], xd[tt * 128:(tt + 1) * 128, :])
        p.bar()
        p.v(lambda e, t2=tt, bu=b: e.scalar_tensor_tensor(
            bu[:], accb[:, t2 * D:(t2 + 1) * D], 0.5, bu[:], AL.mult, AL.add))
        p.bar()
        p.dma(xo[tt * 128:(tt + 1) * 128, :], b[:])
        p.bar()
    return p.fin()



_PROGS = {}
PREP_ORDER = None


def _prep_layout(inp):
    t = lambda a: np.ascontiguousarray(np.asarray(a).T)
    c = lambda a: np.ascontiguousarray(np.asarray(a))
    L = {}
    L["l0a_wq"] = c(inp["L0_attn_Wq"]); L["l0a_wk"] = c(inp["L0_attn_Wk"])
    L["l0a_wv"] = c(inp["L0_attn_Wv"]); L["l0a_wo"] = c(inp["L0_attn_Wo"])
    L["l0f_wih"] = t(inp["L0_ffn_gru_w_ih"])
    L["l0f_whh"] = t(inp["L0_ffn_gru_w_hh"])
    L["l0f_expr"] = t(inp["L0_ffn_expr_w"])
    L["l0f_ctx"] = t(inp["L0_ffn_ctx_w"])
    for e in range(8):
        L[f"l0f_g{e}"] = t(inp["L0_ffn_gate_w"][e])
        L[f"l0f_u{e}"] = t(inp["L0_ffn_up_w"][e])
        L[f"l0f_d{e}"] = t(inp["L0_ffn_down_w"][e])
    L["l0f_sg"] = t(inp["L0_ffn_shared_gate_w"])
    L["l0f_su"] = t(inp["L0_ffn_shared_up_w"])
    L["l0f_sd"] = t(inp["L0_ffn_shared_down_w"])
    L["l1a_wq"] = t(inp["L1_attn_Wq"]); L["l1a_wk"] = t(inp["L1_attn_Wk"])
    L["l1a_wv"] = t(inp["L1_attn_Wv"]); L["l1a_ws"] = t(inp["L1_attn_Ws"])
    L["l2a_wq"] = c(np.transpose(np.asarray(inp["L2_attn_Wq"]), (1, 0, 2)).reshape(D, D))
    L["l2a_wk"] = c(np.transpose(np.asarray(inp["L2_attn_Wk"]), (1, 0, 2)).reshape(D, D))
    L["l2a_wv"] = c(np.transpose(np.asarray(inp["L2_attn_Wv"]), (1, 0, 2)).reshape(D, D))
    L["l2f_wr"] = t(inp["L2_ffn_w_router"])
    for e in range(8):
        L[f"l2f_fc{e}"] = t(inp["L2_ffn_w_fc"][e])
        L[f"l2f_pj{e}"] = t(inp["L2_ffn_w_proj"][e])
    L["l3a_wq"] = c(inp["L3_attn_Wq"]); L["l3a_wk"] = c(inp["L3_attn_Wk"])
    L["l3a_wv"] = c(inp["L3_attn_Wv"]); L["l3a_wo"] = c(inp["L3_attn_Wo"])
    L["l3f_gw"] = t(inp["L3_ffn_gate_w"])
    for e in range(8):
        L[f"l3f_up{e}"] = c(inp["L3_ffn_up_w"][e])
        L[f"l3f_dn{e}"] = c(inp["L3_ffn_down_w"][e])
    L["l3f_su"] = c(inp["L3_ffn_sh_up_w"])
    L["l3f_sd"] = c(inp["L3_ffn_sh_down_w"])
    tern = np.concatenate([t(inp["L1_ffn_w_up"]).reshape(1024, 4096),
                           t(inp["L1_ffn_w_down"]).reshape(1024, 4096)],
                          axis=0).astype(np.float32)
    return L, np.ascontiguousarray(tern)


def _colchunk(b):
    return np.ascontiguousarray(np.asarray(b).reshape(8, 128).T.astype(np.float32))


def _run(name, builder, in_maps, outs):
    if name not in _PROGS:
        _PROGS[name] = builder()
    res = run_bass_kernel_spmd(_PROGS[name], in_maps, core_ids=list(range(NCORES)))
    return [[res.results[c][o] for o in outs] for c in range(NCORES)]


def kernel(**inp):
    C = get_consts()
    L, tern = _prep_layout(inp)
    order = sorted(L.keys())
    sizes = {k: L[k].size for k in order}
    flat = np.concatenate([L[k].reshape(-1).astype(np.float32) for k in order])
    per = -(-flat.size // (NCORES * 128 * 4096)) * 128 * 4096
    pad = np.zeros(per * NCORES - flat.size, dtype=np.float32)
    flat = np.concatenate([flat, pad]).reshape(NCORES, per // 4096, 4096)
    nrows = per // 4096

    prep_maps = []
    for cix in range(NCORES):
        prep_maps.append({"flat_in": flat[cix], "tern_in": tern,
                          "c_ones_row_f": C["c_ones_row_f"],
                          "c_ones_col_f": C["c_ones_col_f"]})
    pres = _run("prep", lambda: build_prep(nrows), prep_maps,
                ["flat_bf", "tern_bf"])
    flat_bf = np.concatenate([np.asarray(pres[cix][0]).reshape(-1)
                              for cix in range(NCORES)])
    W = {}
    off = 0
    for k in order:
        W[k] = flat_bf[off:off + sizes[k]].reshape(L[k].shape)
        off += sizes[k]
    tern_bf = np.asarray(pres[0][1])
    W["l1f_up"] = tern_bf[0:1024, :]
    W["l1f_dn"] = np.ascontiguousarray(tern_bf[1024:2048, :].reshape(4096, 1024))

    x = np.asarray(inp["L0_attn_x"], dtype=np.float32)
    xs = [np.ascontiguousarray(x[cix]) for cix in range(NCORES)]

    def row(v):
        return np.ascontiguousarray(np.asarray(v, dtype=np.float32).reshape(1, -1))

    base_c = {"c_idbf": C["c_idbf"], "c_idf": C["c_idf"],
              "c_ones_row_f": C["c_ones_row_f"],
              "c_ones_col_bf": C["c_ones_col_bf"],
              "c_ones_col_f": C["c_ones_col_f"]}

    def launch(name, builder, extra, outs=("x_out",)):
        maps = []
        for cix in range(NCORES):
            m = dict(base_c)
            m.update(extra)
            m["x"] = xs[cix]
            maps.append(m)
        return _run(name, builder, maps, list(outs))

    def l0ffn():
        nonlocal xs
        ex = {"wihT": W["l0f_wih"], "whhT": W["l0f_whh"],
              "exprT": W["l0f_expr"], "ctxT": W["l0f_ctx"],
              "ctx_b": _colchunk(inp["L0_ffn_ctx_b"])}
        ra = launch("l0a", build_l0a, ex, outs=("cos_dist", "pen_part"))
        pens = np.ascontiguousarray(
            np.array([np.asarray(ra[cix][1]).reshape(-1)[0]
                      for cix in range(NCORES)], dtype=np.float32).reshape(1, 8))
        ex2 = {"pens": pens, "sgw": W["l0f_sg"], "suw": W["l0f_su"],
               "sdw": W["l0f_sd"]}
        for e in range(8):
            ex2[f"gw{e}"] = W[f"l0f_g{e}"]
            ex2[f"uw{e}"] = W[f"l0f_u{e}"]
            ex2[f"dw{e}"] = W[f"l0f_d{e}"]
        maps = []
        for cix in range(NCORES):
            m = dict(base_c)
            m.update(ex2)
            m["x"] = xs[cix]
            m["cos_dist"] = np.ascontiguousarray(np.asarray(ra[cix][0]))
            maps.append(m)
        rb = _run("l0b", build_l0b, maps, ["x_out"])
        xs = [np.asarray(rb[cix][0], dtype=np.float32) for cix in range(NCORES)]

    def att(layer):
        nonlocal xs
        if layer == 0:
            ex = {"wq": W["l0a_wq"], "wk": W["l0a_wk"], "wv": W["l0a_wv"],
                  "wo": W["l0a_wo"], "bq": _colchunk(inp["L0_attn_bq"]),
                  "bk": _colchunk(inp["L0_attn_bk"]), "bv": row(inp["L0_attn_bv"]),
                  "bo": row(inp["L0_attn_bo"])}
        elif layer == 1:
            ex = {"wq": W["l1a_wq"], "wk": W["l1a_wk"], "wv": W["l1a_wv"],
                  "ws": W["l1a_ws"], "bq": row(inp["L1_attn_bq"]),
                  "bk": _colchunk(inp["L1_attn_bk"]), "bv": row(inp["L1_attn_bv"]),
                  "bs": row(inp["L1_attn_bs"])}
        elif layer == 2:
            ex = {"wq": W["l2a_wq"], "wk": W["l2a_wk"], "wv": W["l2a_wv"]}
        else:
            ex = {"wq": W["l3a_wq"], "wk": W["l3a_wk"], "wv": W["l3a_wv"],
                  "wo": W["l3a_wo"], "c_cosrep": C["c_cosrep"],
                  "c_sinrep": C["c_sinrep"], "c_cmask": C["c_cmask"]}
        r = launch(f"att{layer}", lambda: build_att(layer), ex)
        xs = [np.asarray(r[cix][0], dtype=np.float32) for cix in range(NCORES)]

    def ff(layer):
        nonlocal xs
        if layer == 1:
            ex = {"wupq": W["l1f_up"], "wdnq": W["l1f_dn"]}
            r = launch("ff1", build_ff1, ex)
        elif layer == 2:
            ex = {"wrT": W["l2f_wr"]}
            for e in range(8):
                ex[f"fc{e}"] = W[f"l2f_fc{e}"]
                ex[f"pj{e}"] = W[f"l2f_pj{e}"]
            r = launch("ff2", build_ff2, ex)
        else:
            upb = np.ascontiguousarray(
                np.asarray(inp["L3_ffn_up_b"], dtype=np.float32).reshape(
                    8, 8, 128).transpose(2, 0, 1).reshape(128, 64))
            ex = {"gwT": W["l3f_gw"], "upb": upb,
                  "dnb": np.asarray(inp["L3_ffn_down_b"], dtype=np.float32),
                  "suw": W["l3f_su"], "sdw": W["l3f_sd"],
                  "sub": _colchunk(inp["L3_ffn_sh_up_b"]),
                  "sdb": row(inp["L3_ffn_sh_down_b"])}
            for e in range(8):
                ex[f"up{e}"] = W[f"l3f_up{e}"]
                ex[f"dn{e}"] = W[f"l3f_dn{e}"]
            r = launch("ff3", build_ff3, ex)
        xs = [np.asarray(r[cix][0], dtype=np.float32) for cix in range(NCORES)]

    l0ffn(); att(0); l0ffn()
    ff(1); att(1); ff(1)
    ff(2); att(2); ff(2)
    ff(3); att(3); ff(3)
    return np.stack(xs, axis=0)
